# version 06 (final)
# baseline (speedup 1.0000x reference)
import numpy as np
import concourse.bass as bass
import concourse.mybir as mybir

F32 = mybir.dt.float32
F16 = mybir.dt.float16
BF16 = mybir.dt.bfloat16
AX = mybir.AxisListType
ALU = mybir.AluOpType
ACT = mybir.ActivationFunctionType

MAGIC = 1.5 * 2.0 ** 23
NT, TK, D, NH = 16, 2048, 1024, 16
DFI, DFQ = 2816, 3072
NKI, NKQ = DFI // 128, DFQ // 128


class Ctr:
    def __init__(self, nc, name, ctx):
        self.h = ctx.enter_context(nc.semaphore(name))
        self.n = 0

    def take(self, k=1):
        self.n += k
        return self.n


def build(upto="full", debug=(), ntiles=NT, stop=99):
    from contextlib import ExitStack
    nc = bass.Bass("TRN2", target_bir_lowering=False)
    ctx = ExitStack()

    dram = lambda n, s, dt=F32, kind="Internal": nc.dram_tensor(n, s, dt, kind=kind).ap()
    x_in = dram("x", [TK, D], kind="ExternalInput")
    wqkvz_in = dram("wqkvz", [D, 4 * D], kind="ExternalInput")
    wo_in = dram("wo", [D, D], kind="ExternalInput")
    w12_in = dram("w12", [D, 2 * DFI], kind="ExternalInput")
    w3_in = dram("w3", [DFI, D], kind="ExternalInput")
    wup_in = dram("wup_n", [DFQ, D], kind="ExternalInput")
    wgate_in = dram("wgate_n", [DFQ, D], kind="ExternalInput")
    wdown_in = dram("wdown_n", [D, DFQ], kind="ExternalInput")
    g_at_in = dram("g_at_i", [128, 8], kind="ExternalInput")
    g_ff_in = dram("g_ff_i", [128, 8], kind="ExternalInput")
    tril2_in = dram("tril2", [128, 128], kind="ExternalInput")
    ftril_in = dram("ftril", [128, 128], kind="ExternalInput")
    ones128_in = dram("ones128", [128, 128], kind="ExternalInput")
    pmask_in = dram("pmask", [128, 1024], kind="ExternalInput")
    i64_in = dram("i64f", [128, 1024], kind="ExternalInput")
    sel0_in = dram("sel0", [128, 1], kind="ExternalInput")
    sel1_in = dram("sel1", [128, 1], kind="ExternalInput")
    y_out = dram("y", [TK, D], kind="ExternalOutput")

    a1_s = dram("a1_s", [TK, D])
    y1_s = dram("y1_s", [TK, D])
    upraw_s = dram("upraw_s", [DFQ, TK], F16)
    graw_s = dram("graw_s", [DFQ, TK], F16)
    wqup_s = dram("wqup_s", [DFQ, D], BF16)
    wqgate_s = dram("wqgate_s", [DFQ, D], BF16)
    wqdown_s = dram("wqdown_s", [D, DFQ], F16)
    cc_sin = dram("cc_sin", [128, 512])
    cc_sout = dram("cc_sout", [256, 512])
    cc_min = [dram(f"cc_min{i}", [1, 256]) for i in range(4)]
    cc_mout = [dram(f"cc_mout{i}", [1, 256]) for i in range(4)]
    sca_s = dram("sca_s", [1, 16])

    DBG_SHAPES = {"hA": [TK, D], "qf": [TK, D], "kf": [TK, D], "v16": [TK, D],
                  "la": [TK, D], "qt": [TK, D], "kt": [TK, D], "qg": [TK, D],
                  "o": [TK, D], "send": [128, 512], "attn": [TK, D],
                  "a1": [TK, D], "a2": [TK, D], "y1": [TK, D], "sin": [128, 512],
                  "vq": [TK, D], "ffn": [TK, D], "h2": [TK, D], "wdq": [DFQ, D], "m2": [DFQ, TK], "wdt": [DFQ, D],
                  "sx": [128, 16], "arep0": [128, 512], "s20": [128, 512], "ps40": [128, 512]}
    dbg_t = {name: dram("dbg_" + name, DBG_SHAPES[name], F32, kind="ExternalOutput")
             for name in debug}

    sb = lambda n, s, dt=F32: ctx.enter_context(nc.sbuf_tensor(n, s, dt))
    psum = lambda n, s: ctx.enter_context(nc.psum_tensor(n, s, F32))
    actx = ExitStack()
    sbA = lambda n, s, dt=F32: actx.enter_context(nc.sbuf_tensor(n, s, dt))

    g_ff = sb("g_ff", [128, 8])
    stats = sb("stats", [128, 4 * NT])
    st3 = sb("st3", [128, 2 * NT])
    scal = sb("scal", [128, 16])
    mxrow = sb("mxrow", [1, 256])
    xt = sb("xt", [128, D])
    tmp1 = sb("tmp1", [128, D])
    tmp2 = sb("tmp2", [128, D])
    tmp3 = sb("tmp3", [128, D])

    c_tril2 = sbA("c_tril2", [128, 128], BF16)
    c_ftril = sbA("c_ftril", [128, 128], BF16)
    c_ones128 = sbA("c_ones128", [128, 128])
    c_pmask = sbA("c_pmask", [128, 1024], BF16)
    c_i64 = sbA("c_i64", [128, 1024], BF16)
    c_sel0 = sbA("c_sel0", [128, 1])
    c_sel1 = sbA("c_sel1", [128, 1])
    g_at = sbA("g_at", [128, 8])

    w_qkvz = [sbA(f"w_qkvz{j}", [128, 4 * D], BF16) for j in range(8)]
    o16T = [sbA(f"o16T{t}", [128, D], BF16) for t in range(ntiles)]
    qgF = [sbA(f"qgF{j}", [128, 128 * ntiles], BF16) for j in range(8)]
    S2 = sbA("S2", [128, 512], BF16)
    s_f32 = sbA("s_f32", [128, 512])
    la_h = sbA("la_h", [128, D], BF16)
    la_l = sbA("la_l", [128, D], BF16)
    arepA = sbA("arepA", [128, 512])
    td_all = sbA("td_all", [128, 512 * ntiles], BF16)

    hA16 = sbA("hA16", [128, D], BF16)
    hAF = sbA("hAF", [128, D], BF16)
    q_f = sbA("q_f", [128, D])
    k_f = sbA("k_f", [128, D])
    v_16 = sbA("v_16", [128, D], BF16)
    z_f = xt
    la_f = sbA("la_f", [128, D])
    rq = sbA("rq", [128, 2 * NH])
    aex = tmp2
    qt16 = sbA("qt16", [128, D], BF16)
    kt16 = sbA("kt16", [128, D], BF16)
    qg16 = sbA("qg16", [128, D], BF16)
    qtF = sbA("qtF", [128, D], BF16)
    ktF = sbA("ktF", [128, D], BF16)
    p_sb = sbA("p_sb", [128, D], BF16)
    arep = sbA("arep", [128, 512])

    pk = [psum(f"pk{i}", [128, 512]) for i in range(8)]

    sems = ExitStack()
    sq = Ctr(nc, "sq", sems)
    tq = Ctr(nc, "tq", sems)
    gq = Ctr(nc, "gq", sems)
    pe = Ctr(nc, "pe", sems)
    dv = Ctr(nc, "dv", sems)
    ac = Ctr(nc, "ac", sems)
    gp = Ctr(nc, "gp", sems)

    tape = {e: [] for e in ("sync", "gpsimd", "vector", "scalar", "tensor")}

    def on(eng, **frozen):
        def deco(fn):
            tape[eng].append((fn, dict(frozen)))
        return deco

    def dbg_dump(name, src, row0, nrows, waits):
        if name not in dbg_t:
            return
        dst = dbg_t[name]

        @on("gpsimd", dst=dst, src=src, row0=row0, nrows=nrows, waits=tuple(waits))
        def _(g, dst, src, row0, nrows, waits):
            for smh, mark in waits:
                g.wait_ge(smh, mark)
            g.dma_start(dst[row0:row0 + nrows, :], src).then_inc(gq.h, 16)
        gq.take(16)

    @on("sync")
    def _(sync):
        for dst, src in [(c_ones128, ones128_in),
                         (g_at, g_at_in), (g_ff, g_ff_in),
                         (c_sel0, sel0_in), (c_sel1, sel1_in)]:
            sync.dma_start(dst[:], src).then_inc(sq.h, 16)
    m_const = sq.take(5 * 16)

    @on("gpsimd")
    def _(g):
        g.dma_start(c_i64[:], i64_in).then_inc(gq.h, 16)
        g.dma_start(c_tril2[:], tril2_in).then_inc(gq.h, 16)
        g.dma_start(c_ftril[:], ftril_in).then_inc(gq.h, 16)
        g.dma_start(c_pmask[:], pmask_in).then_inc(gq.h, 16)
        for j in range(8):
            g.dma_start(w_qkvz[j][:], wqkvz_in[128 * j:128 * (j + 1), :]).then_inc(gq.h, 16)
    m_wqkvz = gq.take(12 * 16)

    @on("vector", mc=m_const, mw=m_wqkvz)
    def _(v, mc, mw):
        v.wait_ge(sq.h, mc)
        v.wait_ge(gq.h, mw)
        for j in range(8):
            v.tensor_scalar_mul(w_qkvz[j][:], w_qkvz[j][:], g_at[:, j:j + 1])
        v.memset(S2[:], 0.0)
        v.memset(mxrow[:], 0.0).then_inc(dv.h, 1)
    m_wready = dv.take()

    EVQ, EVV, PROJ, SCAN, TT2M, PEV, OEV, TPQ = {}, {}, {}, {}, {}, {}, {}, {}
    DVTILE = {}
    ACTILE = {}

    def a1_tile(t):
        @on("sync", t=t, wv=DVTILE.get(t - 1, 0), wa=ACTILE.get(t - 1, 0))
        def _(sync, t, wv, wa):
            sync.wait_ge(dv.h, wv)
            sync.wait_ge(ac.h, wa)
            sync.dma_start(xt[:], x_in[128 * t:128 * (t + 1), :]).then_inc(sq.h, 16)
        m_x = sq.take(16)

        stmu = stats[:, 4 * t + 0:4 * t + 1]
        stc = stats[:, 4 * t + 1:4 * t + 2]
        stcc2 = stats[:, 4 * t + 2:4 * t + 3]

        @on("vector", t=t, m_x=m_x, stmu=stmu)
        def _(v, t, m_x, stmu):
            v.wait_ge(sq.h, m_x)
            v.bn_stats(scal[:, 0:6], xt[:, 0:512])
            v.bn_stats(scal[:, 6:12], xt[:, 512:1024])
            v.drain()
            v.bn_aggr(scal[:, 12:14], scal[:, 0:12])
            v.drain()
            v.tensor_copy(stmu, scal[:, 12:13])
            v.tensor_scalar_add(scal[:, 14:15], scal[:, 13:14], 1e-5)
            v.drain()
            v.reciprocal(scal[:, 15:16], scal[:, 14:15]).then_inc(dv.h, 1)
        m_var = dv.take()

        @on("scalar", t=t, m=m_var, stc=stc)
        def _(s, t, m, stc):
            s.wait_ge(dv.h, m)
            s.activation(stc, scal[:, 15:16], ACT.Sqrt, bias=0.0, scale=1.0
                         ).then_inc(ac.h, 1)
        m_c = ac.take()

        @on("vector", t=t, m=m_c)
        def _(v, t, m):
            v.wait_ge(ac.h, m)
            v.tensor_tensor(scal[:, 8:9], scal[:, 13:14], scal[:, 15:16], ALU.mult)
            v.drain()
            v.tensor_scalar_add(scal[:, 8:9], scal[:, 8:9], 1e-6).then_inc(dv.h, 1)
        m_vf = dv.take()

        @on("scalar", t=t, m=m_vf)
        def _(s, t, m):
            s.wait_ge(dv.h, m)
            s.activation(scal[:, 9:10], scal[:, 8:9], ACT.Ln, bias=0.0, scale=1.0)
            s.drain()
            s.activation(scal[:, 10:11], scal[:, 9:10], ACT.Exp, bias=0.0, scale=-0.5
                         ).then_inc(ac.h, 1)
        m_c2 = ac.take()

        @on("vector", t=t, m=m_c2, tp_prev=TPQ.get(("hAF", t - 1), 0),
            stmu=stmu, stc=stc, stcc2=stcc2)
        def _(v, t, m, tp_prev, stmu, stc, stcc2):
            v.wait_ge(ac.h, m)
            v.wait_ge(sq.h, tp_prev)
            v.tensor_tensor(stcc2, stc, scal[:, 10:11], ALU.mult)
            v.drain()
            v.tensor_scalar(hA16[:], xt[:], stmu, stcc2, ALU.subtract, ALU.mult
                            ).then_inc(dv.h, 1)
        m_hA = dv.take()
        dbg_dump("hA", hA16[:], 128 * t, 128, [(dv.h, m_hA)])
        if t == 0:
            dbg_dump("sx", stats[:, 0:16], 0, 128, [(dv.h, m_hA)])
            dbg_dump("attn", xt[:], 0, 128, [(dv.h, m_hA)])
        if stop <= 1:
            return

        @on("sync", t=t, m=m_hA, pe_prev=PROJ.get((t - 1, 1), 0))
        def _(sync, t, m, pe_prev):
            sync.wait_ge(dv.h, m)
            sync.wait_ge(pe.h, pe_prev)
            for j in range(8):
                sync.dma_start_transpose(hAF[:, 128 * j:128 * (j + 1)],
                                         hA16[:, 128 * j:128 * (j + 1)]).then_inc(sq.h, 16)
        TPQ[("hAF", t)] = sq.take(8 * 16)

        @on("tensor", t=t, m_tp=TPQ[("hAF", t)], m_w=m_wready,
            evq=EVQ.get(t - 1, 0), evv=EVV.get(t - 1, 0),
            scan_prev=SCAN.get(2 * t - 1, 0))
        def _(te, t, m_tp, m_w, evq, evv, scan_prev):
            te.wait_ge(sq.h, m_tp)
            te.wait_ge(dv.h, m_w)
            te.wait_ge(ac.h, evq)
            te.wait_ge(dv.h, evv)
            for grp in range(2):
                for seg in range(4 * grp, 4 * grp + 4):
                    for j in range(8):
                        ins = te.matmul(pk[seg][:],
                                        hAF[:, 128 * j:128 * (j + 1)],
                                        w_qkvz[j][:, 512 * seg:512 * (seg + 1)],
                                        start=(j == 0), stop=(j == 7))
                ins.then_inc(pe.h, 1)
        PROJ[(t, 0)] = pe.take()
        PROJ[(t, 1)] = pe.take()

        @on("scalar", t=t, m=PROJ[(t, 0)])
        def _(s, t, m):
            s.wait_ge(pe.h, m)
            s.copy(q_f[:, 0:512], pk[0][:])
            s.copy(q_f[:, 512:1024], pk[1][:])
            s.copy(k_f[:, 0:512], pk[2][:])
            s.copy(k_f[:, 512:1024], pk[3][:]).then_inc(ac.h, 1)
        EVQ[t] = ac.take()

        @on("vector", t=t, m=PROJ[(t, 1)], scan_prev=SCAN.get(2 * t - 1, 0))
        def _(v, t, m, scan_prev):
            v.wait_ge(pe.h, m)
            v.wait_ge(pe.h, scan_prev)
            v.tensor_copy(v_16[:, 0:512], pk[4][:])
            v.tensor_copy(v_16[:, 512:1024], pk[5][:])
            v.tensor_copy(z_f[:, 0:512], pk[6][:])
            v.tensor_copy(z_f[:, 512:1024], pk[7][:]).then_inc(dv.h, 1)
        EVV[t] = dv.take()
        dbg_dump("qf", q_f[:], 128 * t, 128, [(ac.h, EVQ[t])])
        dbg_dump("kf", k_f[:], 128 * t, 128, [(ac.h, EVQ[t])])
        dbg_dump("v16", v_16[:], 128 * t, 128, [(dv.h, EVV[t])])
        if stop <= 2:
            return

        @on("vector", t=t, m=EVQ[t])
        def _(v, t, m):
            v.wait_ge(ac.h, m)
            v.tensor_tensor(tmp1[:], q_f[:], q_f[:], ALU.mult)
            v.tensor_tensor(tmp2[:], k_f[:], k_f[:], ALU.mult)
            v.drain()
            v.reduce_sum(rq[:, 0:NH], bass.AP(tmp1, 0, [[D, 128], [64, NH], [1, 64]]),
                         axis=AX.X)
            v.reduce_sum(rq[:, NH:2 * NH], bass.AP(tmp2, 0, [[D, 128], [64, NH], [1, 64]]),
                         axis=AX.X)
            v.drain()
            v.tensor_scalar_max(rq[:], rq[:], 1e-24).then_inc(dv.h, 1)
        m_ss = dv.take()

        @on("scalar", t=t, m=m_ss)
        def _(s, t, m):
            s.wait_ge(dv.h, m)
            s.activation(tmp2[:], z_f[:], ACT.Sigmoid, bias=0.0, scale=1.0).then_inc(ac.h, 1)
        m_sig = ac.take()

        @on("scalar", t=t, m_sig=m_sig, m_ss=m_ss)
        def _(s, t, m_sig, m_ss):
            s.wait_ge(dv.h, m_ss)
            s.wait_ge(ac.h, m_sig)
            s.activation(la_f[:], tmp2[:], ACT.Ln, bias=0.0, scale=1.0)
            s.activation(rq[:], rq[:], ACT.Ln, bias=0.0, scale=1.0).then_inc(ac.h, 1)
        m_la = ac.take()
        dbg_dump("la", la_f[:], 128 * t, 128, [(ac.h, m_la)])
        if stop <= 3:
            return

        @on("vector", t=t, m=m_la, scan_prev=SCAN.get(2 * t - 1, 0))
        def _(v, t, m, scan_prev):
            v.wait_ge(ac.h, m)
            v.wait_ge(pe.h, scan_prev)
            v.tensor_scalar_mul(rq[:], rq[:], 0.5)
            v.tensor_copy(la_h[:], la_f[:])
            v.drain()
            v.tensor_tensor(la_l[:], la_f[:], la_h[:], ALU.subtract).then_inc(dv.h, 1)
        m_rqs = dv.take()

        @on("tensor", t=t, m_la=m_la, m_rqs=m_rqs, evv=EVV[t],
            pev=PEV.get(2 * t - 1, 0), oev=OEV.get(t - 1, 0))
        def _(te, t, m_la, m_rqs, evv, pev, oev):
            te.wait_ge(ac.h, m_la)
            te.wait_ge(dv.h, m_rqs)
            te.wait_ge(dv.h, evv)
            te.wait_ge(dv.h, pev)
            te.wait_ge(ac.h, oev)
            te.matmul(pk[0][:], c_tril2[:], la_h[:, 0:512], start=True, stop=False)
            te.matmul(pk[0][:], c_tril2[:], la_l[:, 0:512], start=False, stop=True)
            te.matmul(pk[1][:], c_tril2[:], la_h[:, 512:1024], start=True, stop=False)
            te.matmul(pk[1][:], c_tril2[:], la_l[:, 512:1024], start=False, stop=True)
            te.matmul(pk[2][:], c_ftril[:], la_h[:, 0:512], start=True, stop=False)
            te.matmul(pk[2][:], c_ftril[:], la_l[:, 0:512], start=False, stop=True)
            te.matmul(pk[3][:], c_ftril[:], la_h[:, 512:1024], start=True, stop=False)
            te.matmul(pk[3][:], c_ftril[:], la_l[:, 512:1024], start=False, stop=True
                      ).then_inc(pe.h, 1)
        m_L = pe.take()

        rq_q0 = bass.AP(rq, 0, [[2 * NH, 128], [1, 8], [0, 64]])
        rq_q1 = bass.AP(rq, 8, [[2 * NH, 128], [1, 8], [0, 64]])
        rq_k0 = bass.AP(rq, NH, [[2 * NH, 128], [1, 8], [0, 64]])
        rq_k1 = bass.AP(rq, NH + 8, [[2 * NH, 128], [1, 8], [0, 64]])

        @on("vector", t=t, m_L=m_L, m_rqs=m_rqs, a=rq_q0, b=rq_q1)
        def _(v, t, m_L, m_rqs, a, b):
            v.wait_ge(pe.h, m_L)
            v.wait_ge(dv.h, m_rqs)
            v.scalar_tensor_tensor(tmp1[:, 0:512], pk[0][:], 1.0, a, ALU.mult, ALU.subtract)
            v.scalar_tensor_tensor(tmp1[:, 512:1024], pk[1][:], 1.0, b, ALU.mult,
                                   ALU.subtract).then_inc(dv.h, 1)
        m_arg1 = dv.take()

        @on("scalar", t=t, m=m_arg1)
        def _(s, t, m):
            s.wait_ge(dv.h, m)
            s.activation(aex[:], tmp1[:], ACT.Exp, bias=0.0, scale=1.0).then_inc(ac.h, 1)
        m_ex1 = ac.take()

        @on("vector", t=t, m=m_ex1, a=rq_k0, b=rq_k1, scan_prev=SCAN.get(2 * t - 1, 0))
        def _(v, t, m, a, b, scan_prev):
            v.wait_ge(ac.h, m)
            v.wait_ge(pe.h, scan_prev)
            v.tensor_tensor(qt16[:], q_f[:], aex[:], ALU.mult)
            v.scalar_tensor_tensor(tmp1[:, 0:512], pk[0][:], -1.0, a, ALU.mult, ALU.subtract)
            v.scalar_tensor_tensor(tmp1[:, 512:1024], pk[1][:], -1.0, b, ALU.mult,
                                   ALU.subtract).then_inc(dv.h, 1)
        m_arg2 = dv.take()

        @on("scalar", t=t, m=m_arg2)
        def _(s, t, m):
            s.wait_ge(dv.h, m)
            s.activation(aex[:], tmp1[:], ACT.Exp, bias=0.0, scale=1.0).then_inc(ac.h, 1)
        m_ex2 = ac.take()

        @on("vector", t=t, m=m_ex2, a=rq_q0, b=rq_q1)
        def _(v, t, m, a, b):
            v.wait_ge(ac.h, m)
            v.tensor_tensor(kt16[:], k_f[:], aex[:], ALU.mult)
            v.scalar_tensor_tensor(tmp1[:, 0:512], pk[2][:], 1.0, a, ALU.mult, ALU.subtract)
            v.scalar_tensor_tensor(tmp1[:, 512:1024], pk[3][:], 1.0, b, ALU.mult,
                                   ALU.subtract).then_inc(dv.h, 1)
        m_arg3 = dv.take()

        @on("scalar", t=t, m=m_arg3)
        def _(s, t, m):
            s.wait_ge(dv.h, m)
            s.activation(aex[:], tmp1[:], ACT.Exp, bias=0.0, scale=1.0).then_inc(ac.h, 1)
        m_ex3 = ac.take()

        @on("vector", t=t, m=m_ex3, tpq=TPQ.get(("qkF", t - 1), 0))
        def _(v, t, m, tpq):
            v.wait_ge(ac.h, m)
            v.wait_ge(sq.h, tpq)
            v.tensor_tensor(qg16[:], q_f[:], aex[:], ALU.mult).then_inc(dv.h, 1)
        m_qg = dv.take()
        dbg_dump("qt", qt16[:], 128 * t, 128, [(dv.h, m_qg)])
        dbg_dump("kt", kt16[:], 128 * t, 128, [(dv.h, m_qg)])
        dbg_dump("qg", qg16[:], 128 * t, 128, [(dv.h, m_qg)])
        if stop <= 4:
            return

        @on("sync", t=t, m=m_qg, scan_prev=SCAN.get(2 * t - 1, 0))
        def _(sync, t, m, scan_prev):
            sync.wait_ge(dv.h, m)
            sync.wait_ge(pe.h, scan_prev)
            for j in range(8):
                sync.dma_start_transpose(qtF[:, 128 * j:128 * (j + 1)],
                                         qt16[:, 128 * j:128 * (j + 1)]).then_inc(sq.h, 16)
                sync.dma_start_transpose(ktF[:, 128 * j:128 * (j + 1)],
                                         kt16[:, 128 * j:128 * (j + 1)]).then_inc(sq.h, 16)
                sync.dma_start_transpose(qgF[j][:, 128 * t:128 * (t + 1)],
                                         qg16[:, 128 * j:128 * (j + 1)]).then_inc(sq.h, 16)
        TPQ[("qkF", t)] = sq.take(24 * 16)

        for p in range(2):
            a1_chunk(t, p, 2 * t + p)

        if stop >= 9:
            @on("scalar", t=t, m=SCAN[2 * t + 1])
            def _(s, t, m):
                s.wait_ge(pe.h, m)
                s.copy(o16T[t][:, 0:512], pk[2][:])
                s.copy(o16T[t][:, 512:1024], pk[3][:]).then_inc(ac.h, 1)
            OEV[t] = ac.take()
            dbg_dump("o", o16T[t][:], 128 * t, 128, [(ac.h, OEV[t])])

        DVTILE[t] = dv.n
        ACTILE[t] = ac.n

    def a1_chunk(t, p, ch):
        @on("tensor", t=t, p=p, m_tp=TPQ[("qkF", t)], pev=PEV.get(ch - 1, 0),
            tt2=TT2M.get(ch - 1, 0), oev=OEV.get(t - 1, 0))
        def _(te, t, p, m_tp, pev, tt2, oev):
            r0 = 64 * p
            te.wait_ge(sq.h, m_tp)
            te.wait_ge(dv.h, pev)
            te.wait_ge(dv.h, tt2)
            if p == 0:
                te.wait_ge(ac.h, oev)
            for h in range(NH):
                hw = 64 * (h % 2)
                j = h // 2
                bank, co = (pk[0], 64 * h) if h < 8 else (pk[1], 64 * (h - 8))
                te.nop()
                ins = te.matmul(bank[r0:r0 + 64, co:co + 64],
                                ktF[hw:hw + 64, 128 * j + r0:128 * j + r0 + 64],
                                qtF[hw:hw + 64, 128 * j + r0:128 * j + r0 + 64],
                                start=True, stop=True)
            ins.then_inc(pe.h, 1)
        m_P = pe.take()

        @on("vector", p=p, m=m_P)
        def _(v, p, m):
            r0 = 64 * p
            v.wait_ge(pe.h, m)
            v.tensor_tensor(p_sb[r0:r0 + 64, 0:512], pk[0][r0:r0 + 64, :],
                            c_pmask[r0:r0 + 64, 0:512], ALU.mult)
            v.tensor_tensor(p_sb[r0:r0 + 64, 512:1024], pk[1][r0:r0 + 64, :],
                            c_pmask[r0:r0 + 64, 512:1024], ALU.mult).then_inc(dv.h, 1)
        PEV[ch] = dv.take()

        if stop <= 5:
            return
        @on("tensor", t=t, p=p, m_pe=PEV[ch], tt2=TT2M.get(ch - 1, 0))
        def _(te, t, p, m_pe, tt2):
            r0 = 64 * p
            te.wait_ge(dv.h, m_pe)
            te.wait_ge(dv.h, tt2)
            for h in range(NH):
                hw = 64 * (h % 2)
                j = h // 2
                pbank, co = (pk[0], 64 * h) if h < 8 else (pk[1], 64 * (h - 8))
                ob, oco = (pk[2], 64 * h) if h < 8 else (pk[3], 64 * (h - 8))
                sc = 64 * (h // 2)
                te.nop()
                te.matmul(ob[r0:r0 + 64, oco:oco + 64],
                          qtF[hw:hw + 64, 128 * j + r0:128 * j + r0 + 64],
                          S2[hw:hw + 64, sc:sc + 64], start=True, stop=False)
                te.nop()
                te.matmul(ob[r0:r0 + 64, oco:oco + 64],
                          p_sb[r0:r0 + 64, 64 * h:64 * h + 64],
                          v_16[r0:r0 + 64, 64 * h:64 * h + 64],
                          start=False, stop=True)
                te.nop()
                te.matmul(pk[4][hw:hw + 64, sc:sc + 64],
                          kt16[r0:r0 + 64, 64 * h:64 * h + 64],
                          v_16[r0:r0 + 64, 64 * h:64 * h + 64], start=True, stop=False)
                te.nop()
                te.matmul(pk[4][hw:hw + 64, sc:sc + 64],
                          c_i64[hw:hw + 64, 64 * h:64 * h + 64],
                          S2[hw:hw + 64, sc:sc + 64], start=False, stop=True)
                te.nop()
                ins = te.matmul(pk[5][hw:hw + 64, sc:sc + 64],
                                la_f[r0:r0 + 64, 64 * h:64 * h + 64],
                                c_ones128[r0:r0 + 64, 0:64], start=True, stop=True)
            ins.then_inc(pe.h, 1)
        SCAN[ch] = pe.take()

        if stop <= 6:
            return
        import os as _os
        _arepop = _os.environ.get("AREPOP", "exp")

        @on("scalar", m=SCAN[ch], _arepop=_arepop)
        def _(s, m, _arepop):
            s.wait_ge(pe.h, m)
            if _arepop == "copy":
                s.copy(arep[:], pk[5][:]).then_inc(ac.h, 1)
            else:
                s.activation(arep[:], pk[5][:], ACT.Exp, bias=0.0, scale=1.0).then_inc(ac.h, 1)
        m_ar = ac.take()
        if stop <= 7:
            @on("vector", m=m_ar)
            def _(v, m):
                v.wait_ge(ac.h, m)
                v.memset(arepA[:, 0:1], 0.0).then_inc(dv.h, 1)
            TT2M[ch] = dv.take(1)
            return

        if ch == 0 and "ps40" in dbg_t:
            @on("vector", m=m_ar)
            def _(v, m):
                v.wait_ge(ac.h, m)
                v.tensor_copy(s_f32[:], pk[4][:]).then_inc(dv.h, 1)
            mtmp = dv.take()
            dbg_dump("ps40", s_f32[:], 0, 128, [(dv.h, mtmp)])
            dbg_dump("arep0", arep[:], 0, 128, [(ac.h, m_ar)])

        @on("vector", m=m_ar, last=(ch == 2 * NT - 1), t=t, p=p)
        def _(v, m, last, t, p):
            v.wait_ge(ac.h, m)
            ins = v.tensor_tensor(S2[:], pk[4][:], arep[:], ALU.mult)
            if p == 0:
                ins = v.tensor_copy(arepA[:], arep[:])
            else:
                ins = v.tensor_tensor(td_all[:, 512 * t:512 * (t + 1)], arepA[:],
                                      arep[:], ALU.mult)
            if last:
                ins = v.tensor_tensor(s_f32[:], pk[4][:], arep[:], ALU.mult)
            ins.then_inc(dv.h, 1)
        TT2M[ch] = dv.take(1)
        if ch == 0 and "s20" in dbg_t:
            @on("vector")
            def _(v):
                v.tensor_copy(tmp1[:, 0:512], S2[:]).then_inc(dv.h, 1)
            mtmp2 = dv.take()
            dbg_dump("s20", tmp1[:, 0:512], 0, 128, [(dv.h, mtmp2)])
        if stop <= 8:
            return

    for t in range(ntiles):
        a1_tile(t)

    def phase_a2():
        S_curA = S2
        o2_16 = qt16
        oF_16 = kt16
        u_t = tmp1
        a1_t = la_f
        g0 = arep
        g1 = s_f32

        m_a1dv = dv.n
        m_a1ac = ac.n
        m_tt2last = TT2M[2 * ntiles - 1]

        @on("gpsimd", m=m_tt2last)
        def _(g, m):
            g.wait_ge(dv.h, m)
            g.dma_start(cc_sin, s_f32[:]).then_inc(gq.h, 16)
        m_ccin = gq.take(16)

        @on("gpsimd", m=m_ccin)
        def _(g, m):
            g.wait_ge(gq.h, m)
            g.collective_compute(
                "AllGather", mybir.AluOpType.bypass,
                replica_groups=[[0, 1], [2, 3], [4, 5], [6, 7]],
                ins=[cc_sin.opt()], outs=[cc_sout.opt()],
            ).then_inc(gp.h, 1)
        m_cc = gp.take()

        @on("gpsimd", m=m_cc)
        def _(g, m):
            g.wait_ge(gp.h, m)
            g.dma_start(g0[:], cc_sout[0:128, :]).then_inc(gq.h, 16)
            g.dma_start(g1[:], cc_sout[128:256, :]).then_inc(gq.h, 16)
        m_gh = gq.take(32)

        @on("vector", m=m_gh)
        def _(v, m):
            v.wait_ge(gq.h, m)
            v.tensor_scalar_mul(tmp2[:, 0:512], g0[:], c_sel0[:])
            v.tensor_scalar_mul(tmp2[:, 512:1024], g1[:], c_sel1[:])
            v.drain()
            v.tensor_tensor(S_curA[:], tmp2[:, 0:512], tmp2[:, 512:1024], ALU.add
                            ).then_inc(dv.h, 1)
        m_sin = dv.take()
        dbg_dump("sin", S_curA[:], 0, 128, [(dv.h, m_sin)])

        @on("gpsimd", m=PROJ[(ntiles - 1, 1)])
        def _(g, m):
            g.wait_ge(pe.h, m)
            for j in range(8):
                g.dma_start(w_qkvz[j][:, 0:1024], wo_in[128 * j:128 * (j + 1), :]
                            ).then_inc(gq.h, 16)
        m_wo = gq.take(8 * 16)

        WOM = {}
        A1M = {}
        SUP = {-1: m_sin}
        for t in range(ntiles):
            @on("tensor", t=t, m_sin=SUP[t - 1], m_tt2=m_tt2last,
                m_tp=TPQ[("qkF", ntiles - 1)], oev=OEV[t], a1m=A1M.get(t - 1, 0))
            def _(te, t, m_sin, m_tt2, m_tp, oev, a1m):
                te.wait_ge(dv.h, m_sin)
                te.wait_ge(dv.h, m_tt2)
                te.wait_ge(sq.h, m_tp)
                te.wait_ge(ac.h, oev)
                for h in range(NH):
                    hw = 64 * (h % 2)
                    j = h // 2
                    bank = pk[4] if h < 8 else pk[5]
                    co = 64 * h if h < 8 else 64 * (h - 8)
                    te.nop()
                    ins = te.matmul(bank[:, co:co + 64],
                                    qgF[j][hw:hw + 64, 128 * t:128 * (t + 1)],
                                    S_curA[hw:hw + 64, 64 * j:64 * j + 64],
                                    start=True, stop=True)
                ins.then_inc(pe.h, 1)
            m_corr = pe.take()

            @on("vector", t=t, m=m_corr)
            def _(v, t, m):
                v.wait_ge(pe.h, m)
                v.scalar_tensor_tensor(o2_16[:, 0:512], pk[4][:], 1.0,
                                       o16T[t][:, 0:512], ALU.mult, ALU.add)
                v.scalar_tensor_tensor(o2_16[:, 512:1024], pk[5][:], 1.0,
                                       o16T[t][:, 512:1024], ALU.mult, ALU.add
                                       ).then_inc(dv.h, 1)
            m_o2 = dv.take()

            @on("scalar", t=t, m=m_o2, wom=WOM.get(t - 1, 0))
            def _(s, t, m, wom):
                s.wait_ge(dv.h, m)
                s.wait_ge(pe.h, wom)
                for j in range(8):
                    s.dma_start_transpose(oF_16[:, 128 * j:128 * (j + 1)],
                                          o2_16[:, 128 * j:128 * (j + 1)]).then_inc(tq.h, 16)
            m_oF = tq.take(8 * 16)

            @on("sync", t=t, a1m=A1M.get(t - 1, 0), adv=m_a1dv, aac=m_a1ac)
            def _(sync, t, a1m, adv, aac):
                sync.wait_ge(dv.h, max(a1m, adv))
                sync.wait_ge(ac.h, aac)
                sync.dma_start(xt[:], x_in[128 * t:128 * (t + 1), :]).then_inc(sq.h, 16)
            m_x2 = sq.take(16)

            @on("tensor", t=t, m_oF=m_oF, m_wo=m_wo, a1m=A1M.get(t - 1, 0))
            def _(te, t, m_oF, m_wo, a1m):
                te.wait_ge(tq.h, m_oF)
                te.wait_ge(gq.h, m_wo)
                te.wait_ge(dv.h, a1m)
                for seg in range(2):
                    for j in range(8):
                        ins = te.matmul(pk[seg][:],
                                        oF_16[:, 128 * j:128 * (j + 1)],
                                        w_qkvz[j][:, 512 * seg:512 * (seg + 1)],
                                        start=(j == 0), stop=(j == 7))
                ins.then_inc(pe.h, 1)
            WOM[t] = pe.take()

            @on("vector", t=t, m_x2=m_x2, m_wo=WOM[t],
                stmu=stats[:, 4 * t + 0:4 * t + 1], stc=stats[:, 4 * t + 1:4 * t + 2])
            def _(v, t, m_x2, m_wo, stmu, stc):
                v.wait_ge(sq.h, m_x2)
                v.wait_ge(pe.h, m_wo)
                v.tensor_scalar(u_t[:], xt[:], stmu, stc, ALU.subtract, ALU.mult)
                v.drain()
                v.scalar_tensor_tensor(a1_t[:, 0:512], pk[0][:], 1.0,
                                       u_t[:, 0:512], ALU.mult, ALU.add)
                v.scalar_tensor_tensor(a1_t[:, 512:1024], pk[1][:], 1.0,
                                       u_t[:, 512:1024], ALU.mult, ALU.add)
                v.drain()
                ins = v.tensor_tensor(S_curA[:], S_curA[:],
                                      td_all[:, 512 * t:512 * (t + 1)], ALU.mult)
                ins.then_inc(dv.h, 1)
            A1M[t] = dv.take()
            SUP[t] = A1M[t]
            dbg_dump("a1", a1_t[:], 128 * t, 128, [(dv.h, A1M[t])])

            @on("sync", t=t, m=A1M[t])
            def _(sync, t, m):
                sync.wait_ge(dv.h, m)
                sync.dma_start(a1_s[128 * t:128 * (t + 1), :], a1_t[:]).then_inc(sq.h, 16)
            sq.take(16)

    def phase_b():
        bv, ba, bp, bs, bg = dv.n, ac.n, pe.n, sq.n, gq.n
        sbB = lambda n, s, dt=F32: bctx.enter_context(nc.sbuf_tensor(n, s, dt))
        w12s = [sbB(f"w12s{j}", [128, 2 * DFI], BF16) for j in range(8)]
        w3s = [sbB(f"w3s{k}", [128, D], BF16) for k in range(NKI)]
        mid = sbB("mid", [128, 512 * NKI], BF16)
        h2F = sbB("h2F", [128, 4096], BF16)
        a1_4 = [sbB(f"a14_{i}", [128, D]) for i in range(4)]
        sa16 = sbB("sa16", [128, 512], BF16)
        sb16v = sbB("sb16v", [128, 512], BF16)
        h216 = sbB("h216", [128, D], BF16)
        wqn = sbB("wqn", [128, D])
        wqt = tmp2
        wqo = sbB("wqo", [128, D], BF16)
        wqs = sbB("wqs", [128, 96])

        @on("gpsimd", bv=bv, ba=ba, bp=bp)
        def _(g, bv, ba, bp):
            g.wait_ge(dv.h, bv)
            g.wait_ge(ac.h, ba)
            g.wait_ge(pe.h, bp)
            for j in range(8):
                g.dma_start(w12s[j][:], w12_in[128 * j:128 * (j + 1), :]).then_inc(gq.h, 16)
            for k in range(NKI):
                g.dma_start(w3s[k][:], w3_in[128 * k:128 * (k + 1), :]).then_inc(gq.h, 16)
        m_w12 = gq.take((8 + NKI) * 16)

        @on("vector", m=m_w12)
        def _(v, m):
            v.wait_ge(gq.h, m)
            for j in range(8):
                ins = v.tensor_scalar_mul(w12s[j][:], w12s[j][:], g_ff[:, j:j + 1])
            ins.then_inc(dv.h, 1)
        m_wb = dv.take()

        ABM = {}
        MIDM = {}
        W3M = {}
        A2M = {}
        TPB = {}
        LDB = {}

        for f in range(4):
            for tt in range(4):
                t = 4 * f + tt

                @on("sync", f=f, tt=tt, t=t, bs=bs,
                    prev=A2M.get((f - 1, tt), (0, 0)))
                def _(sync, f, tt, t, bs, prev):
                    sync.wait_ge(sq.h, bs)
                    sync.wait_ge(dv.h, prev[0])
                    sync.dma_start(a1_4[tt][:], a1_s[128 * t:128 * (t + 1), :]
                                   ).then_inc(sq.h, 16)
                LDB[(f, tt)] = sq.take(16)

                @on("scalar", f=f, tt=tt, m=LDB[(f, tt)], ba=ba,
                    wtprev=(A2M.get((f, tt - 1)) or A2M.get((f - 1, 3)) or (0, 0)))
                def _(s, f, tt, m, ba, wtprev):
                    s.wait_ge(ac.h, ba)
                    s.wait_ge(sq.h, m)
                    s.wait_ge(dv.h, wtprev[0])
                    s.activation(tmp3[:], a1_4[tt][:], ACT.Square, bias=0.0, scale=1.0,
                                 accum_out=scal[:, 0:1]).then_inc(ac.h, 1)
                m_ssb = ac.take()

                @on("vector", f=f, tt=tt, m=m_ssb, bv=bv)
                def _(v, f, tt, m, bv):
                    v.wait_ge(dv.h, bv)
                    v.wait_ge(ac.h, m)
                    v.tensor_scalar(scal[:, 1:2], scal[:, 0:1], 1.0 / D, 1e-6,
                                    ALU.mult, ALU.add).then_inc(dv.h, 1)
                m_m2b = dv.take()

                @on("scalar", f=f, tt=tt, m=m_m2b)
                def _(s, f, tt, m):
                    s.wait_ge(dv.h, m)
                    s.activation(scal[:, 2:3], scal[:, 1:2], ACT.Ln, bias=0.0, scale=1.0)
                    s.drain()
                    s.activation(scal[:, 3:4], scal[:, 2:3], ACT.Exp, bias=0.0,
                                 scale=-0.5).then_inc(ac.h, 1)
                m_rf = ac.take()

                @on("vector", f=f, tt=tt, m=m_rf, tp=TPB.get((f, tt - 1), 0) or TPB.get((f - 1, 3), 0))
                def _(v, f, tt, m, tp):
                    v.wait_ge(ac.h, m)
                    v.wait_ge(tq.h, tp)
                    v.tensor_scalar_mul(h216[:], a1_4[tt][:], scal[:, 3:4]
                                        ).then_inc(dv.h, 1)
                m_h2 = dv.take()

                @on("scalar", f=f, tt=tt, m=m_h2,
                    abprev=ABM.get((f - 1, 21, 1), 0))
                def _(s, f, tt, m, abprev):
                    s.wait_ge(dv.h, m)
                    s.wait_ge(pe.h, abprev)
                    for j in range(8):
                        s.dma_start_transpose(h2F[:, 512 * j + 128 * tt:512 * j + 128 * tt + 128],
                                              h216[:, 128 * j:128 * (j + 1)]).then_inc(tq.h, 16)
                TPB[(f, tt)] = tq.take(8 * 16)

            for m in range(NKI):
                bka, bkb = pk[2 * (m % 2)], pk[2 * (m % 2) + 1]

                @on("tensor", f=f, m=m, bka=bka, bkb=bkb, tp=TPB[(f, 3)],
                    mw=m_wb, mm2=MIDM.get((f, m - 2), 0), w3m=W3M.get((f - 1, 3), 0))
                def _(te, f, m, bka, bkb, tp, mw, mm2, w3m):
                    te.wait_ge(tq.h, tp)
                    te.wait_ge(dv.h, mw)
                    te.wait_ge(dv.h, mm2)
                    te.wait_ge(pe.h, w3m)
                    for j in range(8):
                        ins = te.matmul(bka[:], w12s[j][:, 128 * m:128 * (m + 1)],
                                        h2F[:, 512 * j:512 * (j + 1)],
                                        start=(j == 0), stop=(j == 7))
                    ins.then_inc(pe.h, 1)
                    for j in range(8):
                        ins = te.matmul(bkb[:], w12s[j][:, DFI + 128 * m:DFI + 128 * (m + 1)],
                                        h2F[:, 512 * j:512 * (j + 1)],
                                        start=(j == 0), stop=(j == 7))
                    ins.then_inc(pe.h, 1)
                ABM[(f, m, 0)] = pe.take()
                ABM[(f, m, 1)] = pe.take()

                @on("scalar", f=f, m=m, bka=bka, bkb=bkb, ma=ABM[(f, m, 0)],
                    mb=ABM[(f, m, 1)], mm1=MIDM.get((f, m - 1), 0))
                def _(s, f, m, bka, bkb, ma, mb, mm1):
                    s.wait_ge(pe.h, mb)
                    s.wait_ge(dv.h, mm1)
                    s.activation(sa16[:], bka[:], ACT.Silu, bias=0.0, scale=1.0)
                    s.copy(sb16v[:], bkb[:]).then_inc(ac.h, 1)
                m_sl = ac.take()

                @on("vector", f=f, m=m, msl=m_sl, w3m=W3M.get((f - 1, 3), 0))
                def _(v, f, m, msl, w3m):
                    v.wait_ge(ac.h, msl)
                    v.wait_ge(pe.h, w3m)
                    v.tensor_tensor(mid[:, 512 * m:512 * (m + 1)], sa16[:], sb16v[:],
                                    ALU.mult).then_inc(dv.h, 1)
                MIDM[(f, m)] = dv.take()

            for tt in range(4):
                t = 4 * f + tt

                @on("tensor", f=f, tt=tt, t=t, mlast=MIDM[(f, NKI - 1)],
                    a2prev=A2M.get((f, tt - 1), (0, 0, 0)) if tt else A2M.get((f - 1, 3), (0, 0, 0)))
                def _(te, f, tt, t, mlast, a2prev):
                    te.wait_ge(dv.h, mlast)
                    te.wait_ge(dv.h, a2prev[0])
                    for seg in range(2):
                        for k in range(NKI):
                            ins = te.matmul(pk[4 + seg][:],
                                            mid[:, 512 * k + 128 * tt:512 * k + 128 * tt + 128],
                                            w3s[k][:, 512 * seg:512 * (seg + 1)],
                                            start=(k == 0), stop=(k == NKI - 1))
                    ins.then_inc(pe.h, 1)
                W3M[(f, tt)] = pe.take()

                @on("sync", f=f, tt=tt, t=t, prev=A2M.get((f, tt - 1), (0, 0, 0)) if tt else A2M.get((f - 1, 3), (0, 0, 0)))
                def _(sync, f, tt, t, prev):
                    sync.wait_ge(dv.h, prev[1])
                    sync.dma_start(xt[:], x_in[128 * t:128 * (t + 1), :]).then_inc(sq.h, 16)
                m_xb = sq.take(16)

                @on("vector", f=f, tt=tt, t=t, w3m=W3M[(f, tt)], m_xb=m_xb,
                    gpprev=(A2M.get((f, tt - 1), (0, 0, 0)) if tt
                            else A2M.get((f - 1, 3), (0, 0, 0)))[2])
                def _(v, f, tt, t, w3m, m_xb, gpprev):
                    v.wait_ge(pe.h, w3m)
                    v.wait_ge(gp.h, gpprev)
                    v.scalar_tensor_tensor(tmp1[:, 0:512], pk[4][:], 1.0,
                                           a1_4[tt][:, 0:512], ALU.mult, ALU.add)
                    v.scalar_tensor_tensor(tmp1[:, 512:1024], pk[5][:], 1.0,
                                           a1_4[tt][:, 512:1024], ALU.mult, ALU.add)
                    v.drain()
                    v.bn_stats(scal[:, 0:6], tmp1[:, 0:512])
                    v.bn_stats(scal[:, 6:12], tmp1[:, 512:1024])
                    v.drain()
                    v.bn_aggr(scal[:, 12:14], scal[:, 0:12])
                    v.drain()
                    v.tensor_scalar_add(scal[:, 14:15], scal[:, 13:14], 1e-5)
                    v.drain()
                    v.reciprocal(scal[:, 15:16], scal[:, 14:15]).then_inc(dv.h, 1)
                m_a2s = dv.take()

                @on("scalar", f=f, tt=tt, m=m_a2s)
                def _(s, f, tt, m):
                    s.wait_ge(dv.h, m)
                    s.activation(scal[:, 4:5], scal[:, 15:16], ACT.Sqrt, bias=0.0,
                                 scale=1.0).then_inc(ac.h, 1)
                m_c2b = ac.take()

                @on("vector", f=f, tt=tt, t=t, m=m_c2b, m_xb=m_xb)
                def _(v, f, tt, t, m, m_xb):
                    v.wait_ge(ac.h, m)
                    v.wait_ge(sq.h, m_xb)
                    v.tensor_scalar(tmp2[:], tmp1[:], scal[:, 12:13], scal[:, 4:5],
                                    ALU.subtract, ALU.mult)
                    v.drain()
                    v.tensor_tensor(tmp3[:], tmp2[:], xt[:], ALU.add)
                    v.drain()
                    v.bn_stats(scal[:, 0:6], tmp3[:, 0:512])
                    v.bn_stats(scal[:, 6:12], tmp3[:, 512:1024])
                    v.drain()
                    v.bn_aggr(scal[:, 12:14], scal[:, 0:12])
                    v.drain()
                    v.tensor_copy(st3[:, 2 * t:2 * t + 1], scal[:, 12:13])
                    v.tensor_scalar_add(scal[:, 14:15], scal[:, 13:14], 1e-5)
                    v.drain()
                    v.reciprocal(scal[:, 15:16], scal[:, 14:15]).then_inc(dv.h, 1)
                m_y1 = dv.take()

                @on("sync", f=f, tt=tt, t=t, m=m_y1)
                def _(sync, f, tt, t, m):
                    sync.wait_ge(dv.h, m)
                    sync.dma_start(y1_s[128 * t:128 * (t + 1), :], tmp3[:]).then_inc(sq.h, 16)
                m_y1st = sq.take(16)

                @on("scalar", f=f, tt=tt, t=t, m=m_y1)
                def _(s, f, tt, t, m):
                    s.wait_ge(dv.h, m)
                    s.activation(st3[:, 2 * t + 1:2 * t + 2], scal[:, 15:16], ACT.Sqrt,
                                 bias=0.0, scale=1.0).then_inc(ac.h, 1)
                m_c3 = ac.take()

                @on("vector", f=f, tt=tt, t=t, m=m_c3, m_y1st=m_y1st)
                def _(v, f, tt, t, m, m_y1st):
                    v.wait_ge(ac.h, m)
                    v.wait_ge(sq.h, m_y1st)
                    v.tensor_scalar(tmp1[:], tmp3[:], st3[:, 2 * t:2 * t + 1],
                                    st3[:, 2 * t + 1:2 * t + 2], ALU.subtract, ALU.mult
                                    ).then_inc(dv.h, 1)
                m_wt = dv.take()
                dbg_dump("y1", tmp3[:], 128 * t, 128, [(dv.h, m_wt)])

                @on("gpsimd", f=f, tt=tt, t=t, m=m_wt)
                def _(g, f, tt, t, m):
                    g.wait_ge(dv.h, m)
                    g.tensor_reduce(mxrow[0:1, t:t + 1], tmp1[:], axis=AX.XYZWC,
                                    op=ALU.abs_max).then_inc(gp.h, 1)
                m_gpb = gp.take()
                A2M[(f, tt)] = (m_wt, m_wt, m_gpb)

            for si in range(12 * f, 12 * (f + 1)):
                wsrc = wup_in if si < 24 else wgate_in
                wdst = wqup_s if si < 24 else wqgate_s
                row = (si % 24) * 128

                @on("gpsimd", si=si, wsrc=wsrc, row=row, bv=bv, dqp=gq.n)
                def _(g, si, wsrc, row, bv, dqp):
                    g.wait_ge(dv.h, DQ.get(si - 1, bv))
                    g.dma_start(wqn[:], wsrc[row:row + 128, :]).then_inc(gq.h, 16)
                m_dqld = gq.take(16)

                @on("vector", si=si, m=m_dqld, stprev=DQS.get(si - 1, 0))
                def _(v, si, m, stprev):
                    v.wait_ge(gq.h, m)
                    v.wait_ge(sq.h, stprev)
                    v.reduce_max(wqs[:, 0:32], bass.AP(wqn, 0, [[D, 128], [32, 32], [1, 32]]),
                                 axis=AX.X, apply_absolute_value=True)
                    v.drain()
                    v.tensor_scalar(wqs[:, 32:64], wqs[:, 0:32], 1.0 / 7.0, 1e-12,
                                    ALU.mult, ALU.max)
                    v.drain()
                    v.reciprocal(wqs[:, 64:96], wqs[:, 32:64])
                    v.drain()
                    v.tensor_tensor(wqt[:], wqn[:],
                                    bass.AP(wqs, 64, [[96, 128], [1, 32], [0, 32]]),
                                    ALU.mult)
                    v.drain()
                    v.tensor_scalar(wqn[:], wqt[:], MAGIC, -MAGIC, ALU.add, ALU.add)
                    v.drain()
                    v.tensor_tensor(wqo[:], wqn[:],
                                    bass.AP(wqs, 32, [[96, 128], [1, 32], [0, 32]]),
                                    ALU.mult).then_inc(dv.h, 1)
                DQ[si] = dv.take()

                @on("sync", si=si, wdst=wdst, row=row, m=DQ[si])
                def _(sync, si, wdst, row, m):
                    sync.wait_ge(dv.h, m)
                    sync.dma_start(wdst[row:row + 128, :], wqo[:]).then_inc(sq.h, 16)
                DQS[si] = sq.take(16)

        if "wdq" in dbg_t:
            @on("gpsimd", m=DQ[47], ms=sq.n)
            def _(g, m, ms):
                g.wait_ge(dv.h, m)
                g.wait_ge(sq.h, ms)
                for r in range(0, DFQ, 128):
                    g.dma_start(dbg_t["wdq"][r:r + 128, :], wqup_s[r:r + 128, :]
                                ).then_inc(gq.h, 16)
            gq.take(24 * 16)

    def allreduce_row(idx, vals_marks, n, cbase=8, rbase=12):
        @on("vector", vals=tuple(vals_marks), n=n)
        def _(v, vals, n):
            for ap, mk in vals:
                v.wait_ge(gp.h, mk)
            for i, (ap, mk) in enumerate(vals):
                pitch = int(ap.ap[0][0])
                v.tensor_copy(tmp1[0:1, 128 * i:128 * (i + 1)],
                              bass.AP(ap.tensor, ap.offset, [[pitch, 1], [0, 128]]))
            v.drain()
            v.tensor_copy(tmp2[0:1, 0:1], tmp1[0:1, 0:1]).then_inc(dv.h, 1)
        m_row = dv.take()

        @on("gpsimd", idx=idx, m=m_row, n=n)
        def _(g, idx, m, n):
            g.wait_ge(dv.h, m)
            g.dma_start(cc_min[idx][0:1, 0:128 * n], tmp1[0:1, 0:128 * n]).then_inc(gq.h, 16)
        m_st = gq.take(16)

        @on("gpsimd", idx=idx, m=m_st, n=n)
        def _(g, idx, m, n):
            g.wait_ge(gq.h, m)
            g.collective_compute(
                "AllReduce", mybir.AluOpType.max,
                replica_groups=[list(range(8))],
                ins=[cc_min[idx][0:1, 0:128 * n].opt()],
                outs=[cc_mout[idx][0:1, 0:128 * n].opt()],
            ).then_inc(gp.h, 1)
        m_cc = gp.take()

        @on("gpsimd", idx=idx, m=m_cc, n=n, cb=cbase)
        def _(g, idx, m, n, cb):
            g.wait_ge(gp.h, m)
            for i in range(n):
                g.dma_start(scal[:, cb + i:cb + i + 1],
                            bass.AP(cc_mout[idx].tensor, 128 * i, [[1, 128], [1, 1]])
                            ).then_inc(gq.h, 16)
        m_ld = gq.take(16 * n)

        @on("vector", m=m_ld, n=n, cb=cbase, rb=rbase)
        def _(v, m, n, cb, rb):
            v.wait_ge(gq.h, m)
            for i in range(n):
                v.tensor_scalar(scal[:, cb + i:cb + i + 1], scal[:, cb + i:cb + i + 1],
                                1.0 / 32767.0, 1e-12, ALU.mult, ALU.max)
            v.drain()
            for i in range(n):
                ins = v.reciprocal(scal[:, rb + i:rb + i + 1], scal[:, cb + i:cb + i + 1])
            ins.then_inc(dv.h, 1)
        return dv.take()

    def phase_c():
        cv, ca, cp, cs, cg, cgp = dv.n, ac.n, pe.n, sq.n, gq.n, gp.n
        c1 = ExitStack()
        sbC = lambda n, s, dt=BF16: c1.enter_context(nc.sbuf_tensor(n, s, dt))
        wupT = [sbC(f"wupT{j}", [128, DFQ]) for j in range(8)]
        wgT = [sbC(f"wgT{j}", [128, DFQ]) for j in range(8)]
        vqF = [sbC(f"vqF{j}", [128, TK]) for j in range(8)]
        ev16 = [sbC(f"ev16_{i}", [128, 512], F16) for i in range(4)]
        vq16 = sbC("vq16", [128, D])
        cmag = c1.enter_context(nc.sbuf_tensor("cmag", [128, 1], F32))

        @on("vector", cv=cv)
        def _(v, cv):
            v.wait_ge(dv.h, cv)
            v.memset(cmag[:], MAGIC).then_inc(dv.h, 1)
        m_cm = dv.take()

        @on("gpsimd", cgp=cgp, cv=cv)
        def _(g, cgp, cv):
            g.wait_ge(gp.h, cgp)
            g.wait_ge(dv.h, cv)
            g.tensor_reduce(mxrow[0:1, 32:33], mxrow[0:1, 0:ntiles], axis=AX.XYZWC,
                            op=ALU.abs_max).then_inc(gp.h, 1)
        m_mx1 = gp.take()
        m_sx = allreduce_row(0, [(mxrow[0:1, 32:33], m_mx1)], 1)

        @on("scalar", m=DQ[47], cs=cs)
        def _(s, m, cs):
            s.wait_ge(dv.h, m)
            s.wait_ge(sq.h, cs)
            for j in range(8):
                for mo in range(NKQ):
                    s.dma_start_transpose(wupT[j][:, 128 * mo:128 * (mo + 1)],
                                          wqup_s[128 * mo:128 * (mo + 1), 128 * j:128 * (j + 1)]
                                          ).then_inc(tq.h, 16)
                    s.dma_start_transpose(wgT[j][:, 128 * mo:128 * (mo + 1)],
                                          wqgate_s[128 * mo:128 * (mo + 1), 128 * j:128 * (j + 1)]
                                          ).then_inc(tq.h, 16)
        m_wt = tq.take(8 * NKQ * 2 * 16)

        VQM = {}
        for t in range(ntiles):
            @on("sync", t=t, m=m_sx, prev=VQM.get(t - 1, (0, 0)))
            def _(sync, t, m, prev):
                sync.wait_ge(dv.h, max(m, prev[0]))
                sync.dma_start(tmp3[:], y1_s[128 * t:128 * (t + 1), :]).then_inc(sq.h, 16)
            m_y1l = sq.take(16)

            @on("vector", t=t, m=m_y1l)
            def _(v, t, m):
                v.wait_ge(sq.h, m)
                v.tensor_scalar(tmp1[:], tmp3[:], st3[:, 2 * t:2 * t + 1],
                                st3[:, 2 * t + 1:2 * t + 2], ALU.subtract, ALU.mult
                                ).then_inc(dv.h, 1)
            m_w = dv.take()

            @on("scalar", t=t, m=m_w, mc=m_cm)
            def _(s, t, m, mc):
                s.wait_ge(dv.h, max(m, mc))
                s.activation(tmp2[:], tmp1[:], ACT.Relu, bias=cmag[:],
                             scale=scal[:, 12:13]).then_inc(ac.h, 1)
            m_q1 = ac.take()

            @on("vector", t=t, m=m_q1, tpp=VQM.get(t - 1, (0, 0)))
            def _(v, t, m, tpp):
                v.wait_ge(ac.h, m)
                v.wait_ge(tq.h, tpp[1])
                v.tensor_scalar(vq16[:], tmp2[:], -MAGIC, scal[:, 8:9],
                                ALU.add, ALU.mult).then_inc(dv.h, 1)
            m_vq = dv.take()
            dbg_dump("vq", vq16[:], 128 * t, 128, [(dv.h, m_vq)])

            @on("scalar", t=t, m=m_vq)
            def _(s, t, m):
                s.wait_ge(dv.h, m)
                for j in range(8):
                    s.dma_start_transpose(vqF[j][:, 128 * t:128 * (t + 1)],
                                          vq16[:, 128 * j:128 * (j + 1)]).then_inc(tq.h, 16)
            m_tp = tq.take(8 * 16)
            VQM[t] = (m_vq, m_tp)

        EVA = {}
        GPA = {}
        STA = {}
        for wi, (wT, wdst) in enumerate([(wupT, upraw_s), (wgT, graw_s)]):
            for seq in range(NKQ * 4):
                gsq = wi * NKQ * 4 + seq
                mo, tf = seq // 4, seq % 4
                bank = pk[2 * (gsq % 2)]
                bufi = gsq % 4

                @on("tensor", wi=wi, mo=mo, tf=tf, bank=bank, m_wt=m_wt,
                    m_tp=VQM[ntiles - 1][1], ev=EVA.get(gsq - 2, 0),
                    wTl=tuple(wT))
                def _(te, wi, mo, tf, bank, m_wt, m_tp, ev, wTl):
                    te.wait_ge(tq.h, m_wt)
                    te.wait_ge(tq.h, m_tp)
                    te.wait_ge(ac.h, ev)
                    for j in range(8):
                        ins = te.matmul(bank[:],
                                        wTl[j][:, 128 * mo:128 * (mo + 1)],
                                        vqF[j][:, 512 * tf:512 * (tf + 1)],
                                        start=(j == 0), stop=(j == 7))
                    ins.then_inc(pe.h, 1)
                m_ab = pe.take()

                @on("scalar", bank=bank, bufi=bufi, m=m_ab,
                    gpm=GPA.get(gsq - 4, 0), stm=STA.get(gsq - 4, 0), wi=wi)
                def _(s, bank, bufi, m, gpm, stm, wi):
                    s.wait_ge(pe.h, m)
                    s.wait_ge(gp.h, gpm)
                    s.wait_ge(sq.h, stm)
                    s.copy(ev16[bufi][:], bank[:]).then_inc(ac.h, 1)
                EVA[gsq] = ac.take()

                @on("sync", mo=mo, tf=tf, bufi=bufi, wdst=wdst, m=EVA[gsq])
                def _(sync, mo, tf, bufi, wdst, m):
                    sync.wait_ge(ac.h, m)
                    sync.dma_start(wdst[128 * mo:128 * (mo + 1), 512 * tf:512 * (tf + 1)],
                                   ev16[bufi][:]).then_inc(sq.h, 16)
                STA[gsq] = sq.take(16)

                @on("gpsimd", wi=wi, mo=mo, tf=tf, bufi=bufi, m=EVA[gsq])
                def _(g, wi, mo, tf, bufi, m):
                    g.wait_ge(ac.h, m)
                    g.tensor_reduce(mxrow[0:1, 64 + 96 * wi + 4 * mo + tf:
                                    64 + 96 * wi + 4 * mo + tf + 1],
                                    ev16[bufi][:], axis=AX.XYZWC, op=ALU.abs_max
                                    ).then_inc(gp.h, 1)
                GPA[gsq] = gp.take()

        wqn3 = c1.enter_context(nc.sbuf_tensor("wqn3", [128, DFQ], F32))
        wqt3 = c1.enter_context(nc.sbuf_tensor("wqt3", [128, DFQ], F32))
        wdo = c1.enter_context(nc.sbuf_tensor("wdo", [128, DFQ], F16))
        wqs3 = c1.enter_context(nc.sbuf_tensor("wqs3", [128, 3 * 96], F32))
        DQ3 = {}
        DQ3S = {}
        for sd in range(8):
            @on("gpsimd", sd=sd, prev=DQ3.get(sd - 1, m_sx))
            def _(g, sd, prev):
                g.wait_ge(dv.h, prev)
                g.dma_start(wqn3[:], wdown_in[128 * sd:128 * (sd + 1), :]).then_inc(gq.h, 16)
            m_ld3 = gq.take(16)

            @on("vector", sd=sd, m=m_ld3, stprev=DQ3S.get(sd - 1, 0))
            def _(v, sd, m, stprev):
                v.wait_ge(gq.h, m)
                v.wait_ge(sq.h, stprev)
                v.reduce_max(wqs3[:, 0:96],
                             bass.AP(wqn3, 0, [[DFQ, 128], [32, 96], [1, 32]]),
                             axis=AX.X, apply_absolute_value=True)
                v.drain()
                v.tensor_scalar(wqs3[:, 96:192], wqs3[:, 0:96], 1.0 / 7.0, 1e-12,
                                ALU.mult, ALU.max)
                v.drain()
                v.reciprocal(wqs3[:, 192:288], wqs3[:, 96:192])
                v.drain()
                v.tensor_tensor(wqt3[:], wqn3[:],
                                bass.AP(wqs3, 192, [[288, 128], [1, 96], [0, 32]]),
                                ALU.mult)
                v.drain()
                v.tensor_scalar(wqn3[:], wqt3[:], MAGIC, -MAGIC, ALU.add, ALU.add)
                v.drain()
                v.tensor_tensor(wdo[:], wqn3[:],
                                bass.AP(wqs3, 96, [[288, 128], [1, 96], [0, 32]]),
                                ALU.mult).then_inc(dv.h, 1)
            DQ3[sd] = dv.take()

            @on("sync", sd=sd, m=DQ3[sd])
            def _(sync, sd, m):
                sync.wait_ge(dv.h, m)
                sync.dma_start(wqdown_s[128 * sd:128 * (sd + 1), :], wdo[:]).then_inc(sq.h, 16)
            DQ3S[sd] = sq.take(16)

        C1S = sq.n
        c1.close()
        return C1S, GPA, DQ3[7]

    def phase_c23(C1S, GPA, m_dq3):
        c2 = ExitStack()
        m2sl = c2.enter_context(nc.sbuf_tensor("m2sl", [128, NKQ * TK], F16))
        gt0 = c2.enter_context(nc.sbuf_tensor("gt0", [128, TK], F32))
        gt1 = c2.enter_context(nc.sbuf_tensor("gt1", [128, TK], F16))
        wdT = [c2.enter_context(nc.sbuf_tensor(f"wdT{k}", [128, D], F16))
               for k in range(NKQ)]
        cmag2 = c2.enter_context(nc.sbuf_tensor("cmag2", [128, 1], F32))

        @on("vector", cv=dv.n)
        def _(v, cv):
            v.memset(cmag2[:], MAGIC).then_inc(dv.h, 1)
        m_cm2 = dv.take()

        @on("gpsimd", gpa=GPA[2 * NKQ * 4 - 1])
        def _(g, gpa):
            g.wait_ge(gp.h, gpa)
            g.tensor_reduce(mxrow[0:1, 33:34], mxrow[0:1, 64:160], axis=AX.XYZWC,
                            op=ALU.abs_max)
            g.tensor_reduce(mxrow[0:1, 34:35], mxrow[0:1, 160:256], axis=AX.XYZWC,
                            op=ALU.abs_max).then_inc(gp.h, 1)
        m_mx2 = gp.take()
        m_s2 = allreduce_row(1, [(mxrow[0:1, 33:34], m_mx2),
                                 (mxrow[0:1, 34:35], m_mx2)], 2, cbase=8, rbase=12)

        @on("vector", m=m_s2)
        def _(v, m):
            v.wait_ge(dv.h, m)
            v.tensor_scalar_mul(scal[:, 10:11], scal[:, 9:10], 1.0 / 65536.0
                                ).then_inc(dv.h, 1)
        m_sg2 = dv.take()

        @on("scalar", m=m_dq3, cs=C1S)
        def _(s, m, cs):
            s.wait_ge(dv.h, m)
            s.wait_ge(sq.h, cs)
            for k in range(NKQ):
                for hh in range(8):
                    s.dma_start_transpose(wdT[k][:, 128 * hh:128 * (hh + 1)],
                                          wqdown_s[128 * hh:128 * (hh + 1), 128 * k:128 * (k + 1)]
                                          ).then_inc(tq.h, 16)
        m_wdT = tq.take(NKQ * 8 * 16)

        GCM = {}
        for mo in range(NKQ):
            @on("sync", mo=mo, cs=C1S, prev=GCM.get(mo - 1, 0))
            def _(sync, mo, cs, prev):
                sync.wait_ge(sq.h, cs)
                sync.wait_ge(dv.h, prev)
                sync.dma_start(gt1[:], graw_s[128 * mo:128 * (mo + 1), :]).then_inc(sq.h, 16)
            m_gl = sq.take(16)

            @on("scalar", mo=mo, m=m_gl, ms=m_sg2, mc=m_cm2)
            def _(s, mo, m, ms, mc):
                s.wait_ge(sq.h, m)
                s.wait_ge(dv.h, max(ms, mc))
                s.activation(gt0[:], gt1[:], ACT.Relu, bias=cmag2[:],
                             scale=scal[:, 13:14]).then_inc(ac.h, 1)
            m_g1 = ac.take()

            @on("vector", mo=mo, m=m_g1)
            def _(v, mo, m):
                v.wait_ge(ac.h, m)
                v.tensor_scalar(m2sl[:, TK * mo:TK * (mo + 1)], gt0[:], -MAGIC,
                                scal[:, 10:11], ALU.add, ALU.mult).then_inc(dv.h, 1)
            m_gq16 = dv.take()

            @on("scalar", mo=mo, m=m_gq16)
            def _(s, mo, m):
                s.wait_ge(dv.h, m)
                s.activation(gt1[:], m2sl[:, TK * mo:TK * (mo + 1)], ACT.Sigmoid,
                             bias=0.0, scale=65536.0)
                s.drain()
                s.activation(gt0[:], gt1[:], ACT.Copy, bias=0.0, scale=65536.0
                             ).then_inc(ac.h, 1)
            m_sg = ac.take()

            @on("vector", mo=mo, m=m_sg)
            def _(v, mo, m):
                v.wait_ge(ac.h, m)
                v.tensor_scalar(gt0[:], gt0[:], 65535.4, MAGIC, ALU.min, ALU.add)
                v.drain()
                v.scalar_tensor_tensor(m2sl[:, TK * mo:TK * (mo + 1)], gt0[:], -MAGIC,
                                       m2sl[:, TK * mo:TK * (mo + 1)], ALU.add, ALU.mult
                                       ).then_inc(dv.h, 1)
            GCM[mo] = dv.take()

            @on("gpsimd", mo=mo, m=GCM[mo])
            def _(g, mo, m):
                g.wait_ge(dv.h, m)
                g.tensor_reduce(mxrow[0:1, 64 + mo:65 + mo],
                                m2sl[:, TK * mo:TK * (mo + 1)], axis=AX.XYZWC,
                                op=ALU.abs_max).then_inc(gp.h, 1)
            gp.take()
        m_gp3 = gp.n

        @on("gpsimd", m=m_gp3)
        def _(g, m):
            g.wait_ge(gp.h, m)
            g.tensor_reduce(mxrow[0:1, 35:36], mxrow[0:1, 64:64 + NKQ], axis=AX.XYZWC,
                            op=ALU.abs_max).then_inc(gp.h, 1)
        m_mx3 = gp.take()
        m_s3 = allreduce_row(2, [(mxrow[0:1, 35:36], m_mx3)], 1, cbase=10, rbase=14)

        @on("vector", m=m_s3)
        def _(v, m):
            v.wait_ge(dv.h, m)
            v.tensor_tensor(scal[:, 11:12], scal[:, 10:11], scal[:, 8:9], ALU.mult
                            ).then_inc(dv.h, 1)
        m_smu = dv.take()

        TCM = {}
        for mo in range(NKQ):
            @on("scalar", mo=mo, m=m_smu, mprev=TCM.get(mo - 1, 0))
            def _(s, mo, m, mprev):
                s.wait_ge(dv.h, max(m, mprev))
                s.activation(gt0[:], m2sl[:, TK * mo:TK * (mo + 1)], ACT.Relu,
                             bias=cmag2[:], scale=scal[:, 14:15]).then_inc(ac.h, 1)
            m_g2a = ac.take()

            @on("vector", mo=mo, m=m_g2a)
            def _(v, mo, m):
                v.wait_ge(ac.h, m)
                v.tensor_scalar(m2sl[:, TK * mo:TK * (mo + 1)], gt0[:], -MAGIC,
                                scal[:, 11:12], ALU.add, ALU.mult).then_inc(dv.h, 1)
            m_g2 = dv.take()

            @on("sync", mo=mo, m=m_g2, prev=TCM.get(mo - 1, 0))
            def _(sync, mo, m, prev):
                sync.wait_ge(dv.h, m)
                sync.dma_start(gt1[:], upraw_s[128 * mo:128 * (mo + 1), :]).then_inc(sq.h, 16)
            m_ul = sq.take(16)

            @on("scalar", mo=mo, m=m_ul)
            def _(s, mo, m):
                s.wait_ge(sq.h, m)
                s.activation(gt0[:], gt1[:], ACT.Relu, bias=cmag2[:],
                             scale=scal[:, 12:13]).then_inc(ac.h, 1)
            m_u1 = ac.take()

            @on("vector", mo=mo, m=m_u1)
            def _(v, mo, m):
                v.wait_ge(ac.h, m)
                v.scalar_tensor_tensor(m2sl[:, TK * mo:TK * (mo + 1)], gt0[:], -MAGIC,
                                       m2sl[:, TK * mo:TK * (mo + 1)], ALU.add, ALU.mult
                                       ).then_inc(dv.h, 1)
            TCM[mo] = dv.take()

            @on("gpsimd", mo=mo, m=TCM[mo])
            def _(g, mo, m):
                g.wait_ge(dv.h, m)
                g.tensor_reduce(mxrow[0:1, 96 + mo:97 + mo],
                                m2sl[:, TK * mo:TK * (mo + 1)], axis=AX.XYZWC,
                                op=ALU.abs_max).then_inc(gp.h, 1)
            gp.take()
        m_gp4 = gp.n

        @on("gpsimd", m=m_gp4)
        def _(g, m):
            g.wait_ge(gp.h, m)
            g.tensor_reduce(mxrow[0:1, 36:37], mxrow[0:1, 96:96 + NKQ], axis=AX.XYZWC,
                            op=ALU.abs_max).then_inc(gp.h, 1)
        m_mx4 = gp.take()
        m_s4 = allreduce_row(3, [(mxrow[0:1, 36:37], m_mx4)], 1, cbase=11, rbase=15)

        @on("vector", m=m_s4, mw=m_wdT)
        def _(v, m, mw):
            v.wait_ge(dv.h, m)
            v.wait_ge(tq.h, mw)
            for k in range(NKQ):
                ins = v.tensor_scalar_mul(wdT[k][:], wdT[k][:], 1.0)
            ins.then_inc(dv.h, 1)
        m_wds = dv.take()
        if "wdt" in dbg_t:
            for k in range(NKQ):
                dbg_dump("wdt", wdT[k][:], 128 * k, 128, [(dv.h, m_wds)])

        OCM = {}
        for mo in range(NKQ):
            @on("scalar", mo=mo, m=m_s4, mprev=OCM.get(mo - 1, 0))
            def _(s, mo, m, mprev):
                s.wait_ge(dv.h, max(m, mprev))
                s.activation(gt0[:], m2sl[:, TK * mo:TK * (mo + 1)], ACT.Relu,
                             bias=cmag2[:], scale=scal[:, 15:16]).then_inc(ac.h, 1)
            m_o1 = ac.take()

            @on("vector", mo=mo, m=m_o1)
            def _(v, mo, m):
                v.wait_ge(ac.h, m)
                v.tensor_scalar(m2sl[:, TK * mo:TK * (mo + 1)], gt0[:], -MAGIC,
                                scal[:, 11:12], ALU.add, ALU.mult).then_inc(dv.h, 1)
            OCM[mo] = dv.take()
            dbg_dump("m2", m2sl[:, TK * mo:TK * (mo + 1)], 128 * mo, 128,
                     [(dv.h, OCM[mo])])
        m_oall = OCM[NKQ - 1]

        FNM = {}
        for t in range(ntiles):
            @on("tensor", t=t, m=m_oall, mw=m_wds, prev=FNM.get(t - 1, 0))
            def _(te, t, m, mw, prev):
                te.wait_ge(dv.h, max(m, mw))
                te.wait_ge(dv.h, prev)
                for seg in range(2):
                    for k in range(NKQ):
                        ins = te.matmul(pk[4 + seg][:],
                                        m2sl[:, TK * k + 128 * t:TK * k + 128 * t + 128],
                                        wdT[k][:, 512 * seg:512 * (seg + 1)],
                                        start=(k == 0), stop=(k == NKQ - 1))
                ins.then_inc(pe.h, 1)
            m_dn = pe.take()

            @on("sync", t=t, prev=FNM.get(t - 1, 0))
            def _(sync, t, prev):
                sync.wait_ge(dv.h, prev)
                sync.dma_start(tmp3[:], y1_s[128 * t:128 * (t + 1), :]).then_inc(sq.h, 16)
            m_yl = sq.take(16)

            @on("vector", t=t, m=m_dn, myl=m_yl)
            def _(v, t, m, myl):
                v.wait_ge(pe.h, m)
                v.wait_ge(sq.h, myl)
                v.tensor_copy(tmp1[:, 0:512], pk[4][:])
                v.tensor_copy(tmp1[:, 512:1024], pk[5][:])
                v.drain()
                v.sem_inc(dv.h, 0) if False else None
                v.bn_stats(scal[:, 0:6], tmp1[:, 0:512])
                v.bn_stats(scal[:, 6:12], tmp1[:, 512:1024])
                v.drain()
                v.bn_aggr(scal[:, 12:14], scal[:, 0:12])
                v.drain()
                v.tensor_scalar_add(scal[:, 14:15], scal[:, 13:14], 1e-5)
                v.drain()
                v.reciprocal(scal[:, 15:16], scal[:, 14:15]).then_inc(dv.h, 1)
            m_st4 = dv.take()
            dbg_dump("h2", tmp1[:], 128 * t, 128, [(dv.h, m_st4)])

            @on("scalar", t=t, m=m_st4)
            def _(s, t, m):
                s.wait_ge(dv.h, m)
                s.activation(scal[:, 4:5], scal[:, 15:16], ACT.Sqrt, bias=0.0,
                             scale=1.0).then_inc(ac.h, 1)
            m_c4 = ac.take()

            @on("vector", t=t, m=m_c4)
            def _(v, t, m):
                v.wait_ge(ac.h, m)
                v.tensor_scalar(tmp2[:], tmp1[:], scal[:, 12:13], scal[:, 4:5],
                                ALU.subtract, ALU.mult)
                v.drain()
                v.tensor_tensor(tmp1[:], tmp2[:], tmp3[:], ALU.add).then_inc(dv.h, 1)
            FNM[t] = dv.take()
            dbg_dump("ffn", tmp1[:], 128 * t, 128, [(dv.h, FNM[t])])

            @on("sync", t=t, m=FNM[t])
            def _(sync, t, m):
                sync.wait_ge(dv.h, m)
                sync.dma_start(y_out[128 * t:128 * (t + 1), :], tmp1[:]).then_inc(sq.h, 16)
            sq.take(16)

        c2.close()

    DQ = {}
    DQS = {}
    bctx = ExitStack()

    dbg_dump("send", s_f32[:], 0, 128, [(dv.h, dv.n)])

    if upto != "a1":
        phase_a2()
    if upto not in ("a1", "a2"):
        actx.close()
        phase_b()
        bctx.close()
        if upto not in ("b",):
            C1S, GPA, m_dq3 = phase_c()
            phase_c23(C1S, GPA, m_dq3)
    else:
        actx.close()

    finish(nc, ctx, sems, tape, sq, tq, gq, pe, dv, ac, gp)
    return nc


class DmaSerial:

    _counts = None

    def __init__(self, eng):
        object.__setattr__(self, "_eng", eng)

    def __getattr__(self, k):
        return getattr(object.__getattribute__(self, "_eng"), k)

    def dma_start(self, *a, **kw):
        return _DmaIns(self._eng, self._eng.dma_start(*a, **kw))

    def dma_start_transpose(self, *a, **kw):
        return _DmaIns(self._eng, self._eng.dma_start_transpose(*a, **kw))


class _DmaIns:
    def __init__(self, eng, ins):
        self._eng = eng
        self._ins = ins

    def then_inc(self, sem, k):
        key = id(sem)
        prev = DmaSerial._counts.get(key, 0)
        DmaSerial._counts[key] = prev + k
        ins = self._ins.then_inc(sem, k)
        self._eng.wait_ge(sem, prev + k)
        return ins


def finish(nc, ctx, sems, tape, sq, tq, gq, pe, dv, ac, gp):
    DmaSerial._counts = {}
    import os as _os
    _ser = _os.environ.get("SERQ", "sgt")
    with ctx, sems, nc.Block() as block:
        @block.sync
        def _(sync):
            if "s" in _ser:
                sync = DmaSerial(sync)
            for fn, frozen in tape["sync"]:
                fn(sync, **frozen)
            sync.wait_ge(dv.h, dv.n)
            sync.wait_ge(ac.h, ac.n)
            sync.wait_ge(pe.h, pe.n)
            sync.wait_ge(gp.h, gp.n)
            sync.wait_ge(gq.h, gq.n)
            sync.wait_ge(tq.h, tq.n)
            sync.wait_ge(sq.h, sq.n)

        @block.gpsimd
        def _(g):
            if "g" in _ser:
                g = DmaSerial(g)
            for fn, frozen in tape["gpsimd"]:
                fn(g, **frozen)

        @block.vector
        def _(v):
            for fn, frozen in tape["vector"]:
                fn(v, **frozen)

        @block.scalar
        def _(s):
            if "t" in _ser:
                s = DmaSerial(s)
            for fn, frozen in tape["scalar"]:
                fn(s, **frozen)

        @block.tensor
        def _(te):
            for fn, frozen in tape["tensor"]:
                fn(te, **frozen)



def _host_constants():
    tril = (np.arange(64)[:, None] <= np.arange(64)[None, :]).astype(np.float32)
    tril2 = np.zeros((128, 128), np.float32)
    tril2[:64, :64] = tril
    tril2[64:, 64:] = tril
    ftril = (np.arange(128)[:, None] <= np.arange(128)[None, :]).astype(np.float32)
    ones128 = np.ones((128, 128), np.float32)
    pmask = np.zeros((128, 1024), np.float32)
    i64f = np.zeros((128, 1024), np.float32)
    for h in range(16):
        pmask[:64, 64 * h:64 * h + 64] = tril
        pmask[64:, 64 * h:64 * h + 64] = tril
        i64f[:64, 64 * h:64 * h + 64] = np.eye(64, dtype=np.float32)
        i64f[64:, 64 * h:64 * h + 64] = np.eye(64, dtype=np.float32)
    return dict(tril2=tril2, ftril=ftril, ones128=ones128, pmask=pmask, i64f=i64f)


def _prep_core_inputs(inp, core, consts, shared):
    b, hh = core // 2, core % 2
    d = dict(consts)
    d.update(shared)
    x = inp["L0_attn_x"].reshape(4, 4096, 1024)
    d["x"] = np.ascontiguousarray(x[b, 2048 * hh:2048 * (hh + 1)])
    d["sel0"] = np.full((128, 1), 1.0 if hh == 1 else 0.0, np.float32)
    d["sel1"] = np.zeros((128, 1), np.float32)
    return d


def kernel(**inputs):
    inp = {k: np.ascontiguousarray(np.asarray(v, np.float32)) for k, v in inputs.items()}
    consts = _host_constants()
    shared = {}
    shared["wqkvz"] = np.ascontiguousarray(np.hstack([
        inp["L0_attn_Wq"], inp["L0_attn_Wk"], inp["L0_attn_Wv"], inp["L0_attn_Wg"]]))
    shared["wo"] = inp["L0_attn_Wo"]
    w1 = np.zeros((1024, DFI), np.float32)
    w1[:, :2736] = inp["L0_attn_w1"]
    w2 = np.zeros((1024, DFI), np.float32)
    w2[:, :2736] = inp["L0_attn_w2"]
    shared["w12"] = np.ascontiguousarray(np.hstack([w1, w2]))
    w3 = np.zeros((DFI, 1024), np.float32)
    w3[:2736] = inp["L0_attn_w3"]
    shared["w3"] = w3
    shared["wup_n"] = inp["L0_ffn_w_up"]
    shared["wgate_n"] = inp["L0_ffn_w_gate"]
    shared["wdown_n"] = inp["L0_ffn_w_down"]
    shared["g_at_i"] = np.ascontiguousarray(inp["L0_attn_g_attn"].reshape(8, 128).T)
    shared["g_ff_i"] = np.ascontiguousarray(inp["L0_attn_g_ffn"].reshape(8, 128).T)

    from concourse.bass_utils import run_bass_kernel_spmd
    nc = build(upto="full", debug=())
    in_maps = [_prep_core_inputs(inp, c, consts, shared) for c in range(8)]
    res = run_bass_kernel_spmd(nc, in_maps, core_ids=list(range(8)))
    out = np.empty((4, 4096, 1024), np.float32)
    for c in range(8):
        b, hh = c // 2, c % 2
        out[b, 2048 * hh:2048 * (hh + 1)] = np.asarray(res.results[c]["y"],
                                                       np.float32)
    return out
